# version 06 (final)
# baseline (speedup 1.0000x reference)
import numpy as np
from contextlib import ExitStack

import concourse.bass as bass
import concourse.mybir as mybir
from concourse.bass_utils import run_bass_kernel_spmd

F32 = mybir.dt.float32
I32 = mybir.dt.int32
OP = mybir.AluOpType
ACT = mybir.ActivationFunctionType
AX = mybir.AxisListType

T = 2048
D = 1024
NCH = 8
S = 64
TSUB = 1024
NCHUNK = 4
CS = TSUB // NCHUNK

ENGS = ("t", "v", "s", "g", "q")

DEBUG_STAGE = None


class Emitter:

    def __init__(self, nc):
        self.nc = nc
        self.ops = {k: [] for k in ENGS}
        self.sems = {}
        self.count = {}
        self.waited = {k: {} for k in ENGS}

    def sem(self, key):
        if key not in self.sems:
            self.sems[key] = self.nc.semaphore(key).__enter__()
            self.count[key] = 0
        return self.sems[key]

    def op(self, ek, fn, inc=None, amt=1):
        if inc is None:
            self.ops[ek].append(fn)
            return None
        sem = self.sem(inc)
        self.count[inc] += amt
        val = self.count[inc]
        self.ops[ek].append(lambda e, fn=fn, sem=sem, amt=amt: fn(e).then_inc(sem, amt))
        return (inc, val)

    def wait(self, ek, token):
        if token is None:
            return
        key, val = token
        if self.waited[ek].get(key, 0) >= val:
            return
        self.waited[ek][key] = val
        sem = self.sem(key)
        self.ops[ek].append(lambda e, sem=sem, val=val: e.wait_ge(sem, val))

    def waits(self, ek, *tokens):
        for tk in tokens:
            self.wait(ek, tk)

    def token(self, key):
        return (key, self.count.get(key, 0))

    def marker(self, ek):
        key = ek + "m"
        sem = self.sem(key)
        self.count[key] += 1
        val = self.count[key]
        self.ops[ek].append(lambda e, sem=sem: e.nop().then_inc(sem, 1))
        return (key, val)

    def barrier(self, engines=ENGS):
        toks = {ek: self.marker(ek) for ek in engines}
        for ek in engines:
            for other, tok in toks.items():
                if other != ek:
                    self.wait(ek, tok)

    def flush(self, block):
        def mk(ek):
            def body(eng):
                for fn in self.ops[ek]:
                    fn(eng)
            return body
        block.tensor(mk("t"))
        block.vector(mk("v"))
        block.scalar(mk("s"))
        block.gpsimd(mk("g"))
        block.sync(mk("q"))


class Arena:
    def __init__(self, ap, words):
        self.ap = ap
        self.words = words
        self.off = 0
        self.peak = 0

    def alloc(self, shape):
        n = int(np.prod(shape))
        assert self.off + n <= self.words, f"arena overflow: {self.off}+{n}>{self.words}"
        view = self.ap[:, self.off:self.off + n]
        self.off += n
        self.peak = max(self.peak, self.off)
        if len(shape) > 1:
            pat = " ".join(f"a{i}" for i in range(len(shape)))
            kw = {f"a{i}": int(shape[i]) for i in range(len(shape))}
            view = view.rearrange(f"p ({pat}) -> p {pat}", **kw)
        return view

    def alloc_rows(self, rows, n):
        assert self.off + n <= self.words
        view = self.ap[0:rows, self.off:self.off + n]
        self.off += n
        self.peak = max(self.peak, self.off)
        return view

    def mark(self):
        return self.off

    def reset(self, mark):
        self.off = mark


def build_nc(n_cores):
    nc = bass.Bass("TRN2", target_bir_lowering=False, detect_race_conditions=False)
    KGLOB = 2 * T * n_cores

    def din(name, shape):
        return nc.dram_tensor(name, shape, F32, kind="ExternalInput").ap()

    xT = din("xT", [D, T])
    w_inT = din("w_inT", [D, 2 * D])
    b_in = din("b_in", [2 * D])
    A_in = din("A", [D, S])
    Bm_in = din("Bm", [D, S])
    Cm_in = din("Cm", [D, S])
    Dv_in = din("Dv", [D])
    w_outT = din("w_outT", [D, D])
    b_out = din("b_out", [D])
    g_norm = din("g_norm", [D])
    gwT = din("gwT", [D, 8])
    W1 = din("W1", [8, D, 4096])
    b1 = din("b1", [8, 4096])
    W2 = din("W2", [8, 4096, D])
    b2 = din("b2", [8, D])
    Ws1 = din("Ws1", [D, 4096])
    bs1 = din("bs1", [4096])
    Ws2 = din("Ws2", [4096, D])
    bs2 = din("bs2", [D])
    kqv = din("kqv", [3, D, D])
    phi = din("phi", [D, 2048])
    Wmo = din("Wmo", [64, D, D])
    bmo = din("bmo", [64, D])
    ident_in = din("ident_in", [128, 128])
    sel8_in = din("sel8_in", [8, 1024])

    outT = nc.dram_tensor("outT", [D, T], F32, kind="ExternalOutput").ap()

    sc_hbm = nc.dram_tensor("sc_hbm", [T * 8], F32).ap()
    sc_all_hbm = nc.dram_tensor("sc_all_hbm", [n_cores, T * 8], F32).ap()
    expl_hbm = nc.dram_tensor("expl_hbm", [T, 2048], F32).ap()
    den_hbm = nc.dram_tensor("den_hbm", [T, 2048], F32).ap()
    xT_hbm = nc.dram_tensor("xT_hbm", [T, D], F32).ap()

    es = ExitStack()
    with es:
        em = Emitter(nc)

        def sb(name, shape, dt=F32):
            return es.enter_context(nc.sbuf_tensor(name, shape, dt))[:]

        xres = sb("xres", [128, NCH, T])
        ident = sb("ident", [128, 128])
        sel8 = sb("sel8", [8, 1024])
        ones_col = sb("ones_col", [128, 1])
        ones_row = sb("ones_row", [1, 128])
        A_sb = sb("A_sb", [128, NCH, S])
        Bm_sb = sb("Bm_sb", [128, NCH, S])
        csum_b = sb("csum_b", [128, S])
        Dv_sb = sb("Dv_sb", [128, NCH])
        gn_sb = sb("gn_sb", [128, NCH])
        b_in_sb = sb("b_in_sb", [128, 16])
        b_out_sb = sb("b_out_sb", [128, NCH])
        b1_sb = sb("b1_sb", [128, 8, 32])
        b2_sb = sb("b2_sb", [128, 8, NCH])
        bs1_sb = sb("bs1_sb", [128, 32])
        bs2_sb = sb("bs2_sb", [128, NCH])
        bmo_sb = sb("bmo_sb", [128, 64, NCH])
        h_st = sb("h_st", [128, 4, NCH, S])
        pre_sb = sb("pre_sb", [128, NCH, S])
        xb_sb = sb("xb_sb", [128, NCH, S])
        dvx_sb = sb("dvx_sb", [128, NCH])
        yscr = sb("yscr", [128, NCH, S])
        yred = sb("yred", [128, NCH])
        eps5_sb = sb("eps5_sb", [1, 1])
        row_a = sb("row_a", [1, 512])
        row_b = sb("row_b", [1, 512])
        vspc = sb("vspc", [128, 256])
        sspc = sb("sspc", [1, 1])
        arena_t = sb("arena", [128, 28288])
        ar = Arena(arena_t, 28288)

        psum = [es.enter_context(nc.psum_tensor(f"ps{i}", [128, 512], F32))[:]
                for i in range(8)]
        ps_busy = [None] * 8
        ps_ready = [None] * 8

        block = es.enter_context(nc.Block())

        def dma(ek, out, in_, sem_key, slow=False):
            return em.op(ek, lambda e, o=out, i=in_, sl=slow: e.dma_start(
                out=o, in_=i, allow_slow_non_contiguous=sl), inc=sem_key, amt=16)

        def vspacer(inc=None):
            return em.op("v", lambda e: e.tensor_copy(vspc[:, :], vspc[:, :]),
                         inc=inc)

        def sspacer(inc=None):
            return em.op("s", lambda e: e.copy(sspc[:, :], eps5_sb[:, :]), inc=inc)

        def flush_barrier():
            vspacer()
            vspacer()
            sspacer()
            em.barrier()

        def mmo(out_ap, stat, mov, start, stop, inc=None):
            return em.op("t", lambda e, o=out_ap, a=stat, m=mov, st=start, sp=stop:
                         e.matmul(o, a, m, start=st, stop=sp), inc=inc)

        ld0 = None
        for o, i in [
            (xres[:, :, :], xT.rearrange("(c p) t -> p c t", p=128)),
            (ident[:, :], ident_in[:, :]),
            (sel8[:, :], sel8_in[:, :]),
            (A_sb[:, :, :], A_in.rearrange("(c p) s -> p c s", p=128)),
            (Bm_sb[:, :, :], Bm_in.rearrange("(c p) s -> p c s", p=128)),
            (Dv_sb[:, :], Dv_in.rearrange("(c p) -> p c", p=128)),
            (gn_sb[:, :], g_norm.rearrange("(c p) -> p c", p=128)),
            (b_in_sb[:, :], b_in.rearrange("(c p) -> p c", p=128)),
            (b_out_sb[:, :], b_out.rearrange("(c p) -> p c", p=128)),
            (b1_sb[:, :, :], b1.rearrange("e (c p) -> p e c", p=128)),
            (b2_sb[:, :, :], b2.rearrange("e (c p) -> p e c", p=128)),
            (bs1_sb[:, :], bs1.rearrange("(c p) -> p c", p=128)),
            (bs2_sb[:, :], bs2.rearrange("(c p) -> p c", p=128)),
            (bmo_sb[:, :, :], bmo.rearrange("n (c p) -> p n c", p=128)),
        ]:
            ld0 = dma("q", o, i, "qd", slow=True)
        ld_tok = ld0

        em.op("v", lambda e: e.memset(eps5_sb[:, :], 1e-5))
        em.op("v", lambda e: e.memset(vspc[:, :], 0.0))
        em.op("v", lambda e: e.memset(ones_col[:, :], 1.0))
        em.op("v", lambda e: e.memset(ones_row[:, :], 1.0))
        ones_tok = em.op("v", lambda e: e.memset(h_st.rearrange('p a b c -> p (a b c)'), 0.0), inc="v")

        m0 = ar.mark()
        cm_t = ar.alloc([NCH, S])
        cmld = dma("q", cm_t[:, :, :], Cm_in.rearrange("(c p) s -> p c s", p=128), "qd")
        em.waits("t", cmld, ones_tok)
        tok = None
        for c in range(NCH):
            tok = mmo(psum[0][0:1, 0:S], ones_col[:, :], cm_t[:, c, :],
                      c == 0, c == NCH - 1, inc="t" if c == NCH - 1 else None)
        em.wait("v", tok)
        em.op("v", lambda e: e.tensor_copy(row_a[0:1, 0:S], psum[0][0:1, 0:S]))
        vr = vspacer(inc="v")
        ps_busy[0] = vr
        em.wait("t", vr)
        tok2 = mmo(psum[1][:, 0:S], ones_row[:, :], row_a[0:1, 0:S], True, True, inc="t")
        em.wait("v", tok2)
        csum_tok = em.op("v", lambda e: e.tensor_copy(csum_b[:, :], psum[1][:, 0:S]),
                         inc="v")
        ps_busy[1] = csum_tok
        cm_read_tok = tok
        ar.reset(m0)

        em.waits("v", ld_tok)
        em.waits("s", ld_tok)
        em.waits("t", ld_tok)
        em.waits("g", ld_tok)

        m_l0a = ar.mark()
        wslot = [ar.alloc([1024]) for _ in range(2)]
        wslot_free = [cm_read_tok, cm_read_tok]
        xn = [ar.alloc([NCH, CS])]
        xn_free = [None]
        xz = [ar.alloc([16, CS]) for _ in range(2)]
        xz_free = [None, None]
        ybuf = ar.alloc([NCH, CS])
        yrbuf = ar.alloc([CS, NCH])
        ysg = ar.alloc([NCH, CS])
        sqb = ar.alloc([NCH, CS])
        ln_sq = sqb
        wos = [ar.alloc([NCH, 128]) for _ in range(2)]
        wos_free = [None, None]

        scan_h_tok = ones_tok
        sq_read_tok = None
        ysg_free = None
        ybuf_free = None
        s_step = 0
        wslot_i = 0
        wout_loaded = None
        ln_jobs = []

        def next_wslot():
            nonlocal wslot_i
            sl = wslot_i % 2
            wslot_i += 1
            return sl

        for ci in range(NCHUNK):
            u0 = ci * CS
            xz_t = xz[ci % 2]
            xn_t = xn[0]

            xsub = xres[:, :, u0:u0 + CS]
            em.waits("s", sq_read_tok)
            em.op("s", lambda e, xs=xsub: e.activation(
                sqb[:, :, :], xs, ACT.Square))
            sq_tok = sspacer(inc="s")
            em.waits("t", sq_tok, ps_busy[6])
            rms_tok = None
            for c in range(NCH):
                rms_tok = mmo(psum[6][0:1, 0:CS], ones_col[:, :], sqb[:, c, :],
                              c == 0, c == NCH - 1,
                              inc="t" if c == NCH - 1 else None)
            ps_ready[6] = rms_tok
            sq_read_tok = rms_tok
            em.wait("v", rms_tok)
            em.op("v", lambda e: e.tensor_scalar(
                row_a[0:1, 0:CS], psum[6][0:1, 0:CS], 1.0 / D, 1e-6, OP.mult, OP.add))
            t1 = vspacer(inc="v")
            ps_busy[6] = t1
            em.wait("s", t1)
            em.op("s", lambda e: e.activation(
                row_b[0:1, 0:CS], row_a[0:1, 0:CS], ACT.Sqrt))
            t2 = sspacer(inc="s")
            em.wait("v", t2)
            em.op("v", lambda e: e.reciprocal(
                row_a[0:1, 0:CS], row_b[0:1, 0:CS]))
            t3 = vspacer(inc="v")
            em.waits("t", t3, ps_busy[7])
            t4 = mmo(psum[7][:, 0:CS], ones_row[:, :], row_a[0:1, 0:CS], True, True,
                     inc="t")
            ps_ready[7] = t4
            em.waits("v", t4, xn_free[0])
            for c in range(NCH):
                em.op("v", lambda e, c=c, xs=xsub, xnt=xn_t:
                      e.scalar_tensor_tensor(
                          xnt[:, c, :], xs[:, c, :], gn_sb[:, c:c + 1],
                          psum[7][:, 0:CS], OP.mult, OP.mult))
            xn_tok = vspacer(inc="v")
            ps_busy[7] = xn_tok

            em.waits("t", xn_tok)
            for grp in range(2):
                loads = {}
                sl = next_wslot()
                em.waits("q", wslot_free[sl])
                loads[0] = (sl, dma("q", wslot[sl][:, :],
                                    w_inT[bass.ts(0, 128),
                                          bass.ts(grp, 1024)], "qd"))
                for c in range(NCH):
                    if c + 1 < NCH:
                        sl = next_wslot()
                        em.waits("q", wslot_free[sl])
                        loads[c + 1] = (sl, dma("q", wslot[sl][:, :],
                                                w_inT[bass.ts(c + 1, 128),
                                                      bass.ts(grp, 1024)], "qd"))
                    sl, ltk = loads[c]
                    em.wait("t", ltk)
                    for j in range(8):
                        eo = grp * 8 + j
                        if c == 0:
                            em.waits("t", ps_busy[j])
                        mtok = mmo(psum[j][:, 0:CS],
                                   wslot[sl][:, bass.ts(j, 128)], xn_t[:, c, :],
                                   c == 0, c == NCH - 1,
                                   inc="t" if j == 7 else None)
                    wslot_free[sl] = mtok
                grp_done = mtok
                for j in range(8):
                    ps_ready[j] = grp_done
                if grp == 1:
                    xn_free[0] = grp_done
                em.waits("s", xz_free[ci % 2])
                for j in range(8):
                    eo = grp * 8 + j
                    em.wait("s", ps_ready[j])
                    func = ACT.Identity if eo < 8 else ACT.Silu
                    etok = em.op("s", lambda e, j=j, eo=eo, f=func, xzt=xz_t:
                                 e.activation(xzt[:, eo, :], psum[j][:, 0:CS], f,
                                              bias=b_in_sb[:, eo:eo + 1]), inc="s")
                    ps_busy[j] = etok
            xz_ready = sspacer(inc="s")

            em.waits("v", xz_ready, csum_tok, ybuf_free)

            def emit_y(u):
                hs = h_st[:, (s_step + u) % 4, :, :]
                cb = csum_b[:, :].unsqueeze(1).broadcast_to([128, NCH, S])
                em.op("v", lambda e, hs=hs, cb=cb: e.tensor_tensor(
                    yscr[:, :, :], hs, cb, OP.mult))
                em.op("v", lambda e, u=u: e.tensor_reduce(
                    yrbuf[:, u, :], yscr[:, :, :], AX.X, OP.add))

            for u in range(CS):
                xt = xz_t[:, 0:8, u]
                xtb = xt.unsqueeze(2).broadcast_to([128, NCH, S])
                em.wait("v", scan_h_tok)
                hprev = h_st[:, (s_step + u + 3) % 4, :, :]
                em.op("v", lambda e, hp=hprev: e.tensor_tensor(
                    pre_sb[:, :, :], hp, A_sb[:, :, :], OP.mult))
                em.op("v", lambda e, xb=xtb: e.tensor_tensor(
                    xb_sb[:, :, :], Bm_sb[:, :, :], xb, OP.mult))
                vtok = em.op("v", lambda e: e.tensor_tensor(
                    pre_sb[:, :, :], pre_sb[:, :, :], xb_sb[:, :, :], OP.add),
                    inc="v")
                if u > 0:
                    emit_y(u - 1)
                em.wait("s", vtok)
                hcur = h_st[:, (s_step + u) % 4, :, :]
                scan_h_tok = em.op("s", lambda e, hc=hcur: e.activation(
                    hc, pre_sb[:, :, :], ACT.Tanh), inc="s")
            em.wait("v", scan_h_tok)
            emit_y(CS - 1)
            dvb = Dv_sb[:, :].unsqueeze(2).broadcast_to([128, NCH, CS])
            em.op("v", lambda e, xzt=xz_t, dvb=dvb: e.tensor_tensor(
                ybuf[:, :, :], xzt[:, 0:8, :], dvb, OP.mult))
            scan_done = em.op("v", lambda e: e.tensor_tensor(
                ybuf[:, :, :], ybuf[:, :, :],
                yrbuf.rearrange("p u c -> p c u"), OP.add), inc="v")
            s_step += CS

            em.waits("v", scan_done, ysg_free)
            ysg_tok = em.op("v", lambda e, xzt=xz_t: e.tensor_tensor(
                ysg[:, :, :], ybuf[:, :, :], xzt[:, 8:16, :], OP.mult), inc="v")
            ybuf_free = ysg_tok
            xz_free[ci % 2] = ysg_tok

            em.waits("t", ysg_tok)
            wo_last = None
            for eo in range(NCH):
                sl = eo % 2
                em.waits("q", wos_free[sl])
                wltk = dma("q", wos[sl][:, :, :],
                           w_outT[:, bass.ts(eo, 128)].rearrange(
                               "(c p) e -> p c e", p=128), "qd")
                em.wait("t", wltk)
                bk = eo % 4
                em.waits("t", ps_busy[bk])
                wtok = None
                for c in range(NCH):
                    stat = wos[sl][:, c, :]
                    wtok = mmo(psum[bk][:, 0:CS], stat, ysg[:, c, :],
                               c == 0, c == NCH - 1,
                               inc="t" if c == NCH - 1 else None)
                wos_free[sl] = wtok
                ps_ready[bk] = wtok
                wo_last = wtok
                em.wait("v", wtok)
                xeven = xres[:, eo, u0:u0 + CS]
                xodd = xres[:, eo, TSUB + u0:TSUB + u0 + CS]
                em.op("v", lambda e, xc=xeven, bk=bk, eo=eo:
                      e.scalar_tensor_tensor(
                          xc, psum[bk][:, 0:CS], b_out_sb[:, eo:eo + 1], xc,
                          OP.add, OP.add))
                vtok = em.op("v", lambda e, xc=xodd, bk=bk, eo=eo:
                             e.scalar_tensor_tensor(
                                 xc, psum[bk][:, 0:CS], b_out_sb[:, eo:eo + 1], xc,
                                 OP.add, OP.add), inc="v")
                ps_busy[bk] = vtok
            ysg_free = wo_last
            ln_jobs.append((u0, CS, em.token("v")))
            ln_jobs.append((TSUB + u0, CS, em.token("v")))

        def ln_inplace(cols0, ncols, pre_tok, lnsq=None):
            if lnsq is None:
                lnsq = ln_sq
            assert ncols <= 512
            em.waits("s", pre_tok)
            em.op("s", lambda e, c0=cols0, n=ncols, q=lnsq: e.activation(
                q[:, :, 0:n], xres[:, :, c0:c0 + n], ACT.Square))
            stok = sspacer(inc="s")
            em.waits("t", pre_tok, stok, ps_busy[6], ps_busy[7])
            tks = tkq = None
            for c in range(NCH):
                tks = mmo(psum[6][0:1, 0:ncols], ones_col[:, :],
                          xres[:, c, cols0:cols0 + ncols], c == 0, c == NCH - 1,
                          inc="t" if c == NCH - 1 else None)
            for c in range(NCH):
                tkq = mmo(psum[7][0:1, 0:ncols], ones_col[:, :], lnsq[:, c, 0:ncols],
                          c == 0, c == NCH - 1, inc="t" if c == NCH - 1 else None)
            ps_ready[6], ps_ready[7] = tks, tkq
            em.waits("v", tks, tkq)
            v1 = em.op("v", lambda e, n=ncols: e.tensor_scalar(
                row_a[0:1, 0:n], psum[6][0:1, 0:n], 1.0 / D, None, OP.mult), inc="v")
            ps_busy[6] = v1
            vspacer()
            em.op("v", lambda e, n=ncols: e.tensor_tensor(
                row_b[0:1, 0:n], row_a[0:1, 0:n], row_a[0:1, 0:n], OP.mult))
            vspacer()
            em.op("v", lambda e, n=ncols: e.scalar_tensor_tensor(
                row_b[0:1, 0:n], psum[7][0:1, 0:n], 1.0 / D, row_b[0:1, 0:n],
                OP.mult, OP.subtract))
            v2 = vspacer(inc="v")
            ps_busy[7] = v2
            em.wait("s", v2)
            em.op("s", lambda e, n=ncols: e.activation(
                row_b[0:1, 0:n], row_b[0:1, 0:n], ACT.Sqrt,
                bias=eps5_sb[0:1, 0:1]))
            s2 = sspacer(inc="s")
            em.wait("v", s2)
            em.op("v", lambda e, n=ncols: e.reciprocal(
                row_b[0:1, 0:n], row_b[0:1, 0:n]))
            v3 = vspacer(inc="v")
            em.waits("t", v3, ps_busy[4], ps_busy[5])
            t4 = mmo(psum[4][:, 0:ncols], ones_row[:, :], row_a[0:1, 0:ncols],
                     True, True, inc="t")
            t5 = mmo(psum[5][:, 0:ncols], ones_row[:, :], row_b[0:1, 0:ncols],
                     True, True, inc="t")
            ps_ready[4], ps_ready[5] = t4, t5
            em.waits("v", t4, t5)
            out_tok = None
            for c in range(NCH):
                em.op("v", lambda e, c=c, c0=cols0, n=ncols: e.tensor_tensor(
                    xres[:, c, c0:c0 + n], xres[:, c, c0:c0 + n], psum[4][:, 0:n],
                    OP.subtract))
                out_tok = em.op("v", lambda e, c=c, c0=cols0, n=ncols: e.tensor_tensor(
                    xres[:, c, c0:c0 + n], xres[:, c, c0:c0 + n], psum[5][:, 0:n],
                    OP.mult), inc="v" if c == NCH - 1 else None)
            ps_busy[4] = ps_busy[5] = out_tok
            return out_tok

        for cols0, ncols, rtok in ln_jobs:
            ln_inplace(cols0, ncols, rtok)

        em.barrier()

        if isinstance(DEBUG_STAGE, str) and DEBUG_STAGE.startswith("dump1:"):
            name = DEBUG_STAGE.split(":")[1]
            buf = {"xn": xn[0], "xz": xz[1], "ybuf": ybuf, "ysg": ysg,
                   "sqb": sqb}[name]
            flat3 = outT.rearrange("(c p) t -> p c t", p=128)
            bufl = buf.rearrange("p a b -> p (a b)") if buf.ndim == 3 else buf
            nw = int(np.prod(buf.shape[1:]))
            st = None
            for i in range((nw + 2047) // 2048):
                n = min(2048, nw - i * 2048)
                st = dma("q", flat3[:, i, 0:n], bufl[:, i * 2048:i * 2048 + n],
                         "qd")
            em.waits("q", st)
            em.flush(block)
            return nc
        ar.reset(m_l0a)

        if DEBUG_STAGE == "s1":
            st = dma("q", outT.rearrange("(c p) t -> p c t", p=128), xres[:, :, :],
                     "qd")
            em.waits("q", st)
            em.flush(block)
            return nc

        TT2 = 512
        NTT = T // TT2
        m_ffn = ar.mark()
        gateT = ar.alloc_rows(8, 2048)

        m_phA = ar.mark()
        scoresF = ar.alloc_rows(8, 2048)
        sc_tm = ar.alloc([16, 8])
        gate_tm = ar.alloc([16, 8])
        srow = ar.alloc([16])
        sc_all = ar.alloc([n_cores, 128])
        bis = ar.alloc([8])
        gwslot = ar.alloc([NCH, 8])

        gwtok = dma("q", gwslot[:, :, :], gwT.rearrange("(c p) e -> p c e", p=128),
                    "qd", slow=True)
        em.waits("t", gwtok)
        for tt in range(4):
            bk = tt % 2
            em.waits("t", ps_busy[bk])
            stok = None
            for c in range(NCH):
                stok = mmo(psum[bk][0:8, :], gwslot[:, c, :],
                           xres[:, c, bass.ts(tt, 512)], c == 0, c == NCH - 1,
                           inc="t" if c == NCH - 1 else None)
            ps_ready[bk] = stok
            em.wait("v", stok)
            vtok = em.op("v", lambda e, tt=tt, bk=bk: e.tensor_copy(
                scoresF[0:8, bass.ts(tt, 512)], psum[bk][0:8, :]), inc="v")
            ps_busy[bk] = vtok
        sc_f_tok = em.token("v")
        em.waits("t", sc_f_tok)
        for tb in range(16):
            bk = 2 + tb % 2
            em.waits("t", ps_busy[bk])
            ttok = em.op("t", lambda e, tb=tb, bk=bk: e.transpose(
                psum[bk][0:128, 0:8], scoresF[0:8, bass.ts(tb, 128)],
                ident[0:8, 0:8]), inc="t")
            ps_ready[bk] = ttok
            em.wait("v", ttok)
            vtok = em.op("v", lambda e, tb=tb, bk=bk: e.tensor_copy(
                sc_tm[:, tb, :], psum[bk][0:128, 0:8]), inc="v")
            ps_busy[bk] = vtok
        sc_tm_tok = vspacer(inc="v")
        em.waits("s", sc_tm_tok)
        etok = em.op("s", lambda e: e.activation(
            sc_tm[:, :, :], sc_tm[:, :, :], ACT.Exp), inc="s")
        em.wait("v", etok)
        em.op("v", lambda e: e.tensor_reduce(srow[:, :], sc_tm[:, :, :], AX.X, OP.add))
        vspacer()
        em.op("v", lambda e: e.reciprocal(srow[:, :], srow[:, :]))
        vspacer()
        smx_tok = em.op("v", lambda e: e.tensor_tensor(
            sc_tm[:, :, :], sc_tm[:, :, :],
            srow[:, :].unsqueeze(2).broadcast_to([128, 16, 8]), OP.mult), inc="v")

        em.waits("g", smx_tok)
        bounce_tok = dma("g", sc_hbm.rearrange("(p w) -> p w", p=128),
                         sc_tm[:, :, :].rearrange("p a b -> p (a b)"), "gd")
        em.waits("g", bounce_tok)
        cc1 = em.op("g", lambda e: e.collective_compute(
            "AllGather", OP.bypass, replica_groups=[list(range(n_cores))],
            ins=[sc_hbm.opt()], outs=[sc_all_hbm.opt()]), inc="cc")
        em.waits("g", cc1)
        scall_tok = dma("g", sc_all[:, :, :],
                        sc_all_hbm.rearrange("n (p w) -> p n w", p=128), "gd")

        lopq = [bis[:, 0:1], bis[:, 2:3]]
        hipq = [bis[:, 1:2], bis[:, 3:4]]
        mid = bis[:, 4:5]
        msk = bis[:, 5:6]
        cntb = bis[:, 6:7]
        cmpbuf = ar.alloc([n_cores * 128])
        em.op("v", lambda e: e.memset(lopq[0], 0.0))
        bist = em.op("v", lambda e: e.memset(hipq[0], 1.0), inc="v")
        em.waits("v", scall_tok)
        NBIS = 64
        for it in range(NBIS):
            lo, hi = lopq[it % 2], hipq[it % 2]
            lon, hin = lopq[(it + 1) % 2], hipq[(it + 1) % 2]
            em.op("v", lambda e, lo=lo, hi=hi: e.tensor_tensor(mid, lo, hi, OP.add))
            vspacer()
            em.op("v", lambda e: e.tensor_scalar(mid, mid, 0.5, None, OP.mult))
            vspacer()
            em.op("v", lambda e: e.tensor_scalar(
                cmpbuf[:, :], sc_all[:, :, :].rearrange("p a b -> p (a b)"),
                mid[:, 0:1], 0.0, OP.is_ge, OP.add, accum_out=cntb))
            vtok = vspacer(inc="v")
            em.waits("t", vtok, ps_busy[0])
            t1 = mmo(psum[0][0:1, 0:1], ones_col[:, :], cntb, True, True, inc="t")
            em.wait("v", t1)
            em.op("v", lambda e: e.tensor_copy(row_a[0:1, 0:1],
                                               psum[0][0:1, 0:1]))
            v2 = vspacer(inc="v")
            ps_busy[0] = v2
            em.waits("t", v2, ps_busy[1])
            t2 = mmo(psum[1][:, 0:1], ones_row[:, :], row_a[0:1, 0:1], True, True,
                     inc="t")
            em.wait("v", t2)
            em.op("v", lambda e: e.tensor_scalar(
                msk.bitcast(I32), psum[1][:, 0:1], float(KGLOB), None, OP.is_ge))
            vspacer()
            em.op("v", lambda e, lo=lo, lon=lon: e.select(
                lon, msk.bitcast(I32), mid, lo))
            em.op("v", lambda e, hi=hi, hin=hin: e.select(
                hin, msk.bitcast(I32), hi, mid))
            v3 = vspacer(inc="v")
            ps_busy[1] = v3
        lo = lopq[NBIS % 2]
        if DEBUG_STAGE == "s2":
            dbg_tok = em.op("v", lambda e: e.engine_nop(), inc="v")
            em.waits("q", dbg_tok)
            dma("q", den_hbm[0:128, 0:8], bis[:, 0:8], "qd")
            dma("q", den_hbm[0:128, 8:8 + n_cores * 128], 
                sc_all[:, :, :].rearrange("p a b -> p (a b)"), "qd")
        em.op("v", lambda e: e.tensor_scalar(
            gate_tm[:, :, :], sc_tm[:, :, :], lo[:, 0:1], None, OP.is_ge))
        vspacer()
        gtok = em.op("v", lambda e: e.tensor_tensor(
            gate_tm[:, :, :], gate_tm[:, :, :], sc_tm[:, :, :], OP.mult), inc="v")
        em.waits("t", gtok)
        for tb in range(16):
            bk = tb % 2
            em.waits("t", ps_busy[bk])
            ttok = em.op("t", lambda e, tb=tb, bk=bk: e.transpose(
                psum[bk][0:8, 0:128], gate_tm[:, tb, :], ident[0:128, 0:128]),
                inc="t")
            ps_ready[bk] = ttok
            em.wait("v", ttok)
            vtok = em.op("v", lambda e, tb=tb, bk=bk: e.tensor_copy(
                gateT[0:8, bass.ts(tb, 128)], psum[bk][0:8, 0:128]), inc="v")
            ps_busy[bk] = vtok
        gateT_tok = vspacer(inc="v")
        em.barrier()
        if isinstance(DEBUG_STAGE, str) and DEBUG_STAGE.startswith("dump2:"):
            name = DEBUG_STAGE.split(":")[1]
            flat3 = outT.rearrange("(c p) t -> p c t", p=128)
            if name == "bis":
                st = dma("q", flat3[:, 0, 0:8], bis[:, 0:8], "qd")
                st = dma("q", flat3[:, 1, 0:128], sc_all[:, 0, :], "qd")
            elif name == "gateT":
                st = dma("q", flat3[0:8, 0, 0:2048], gateT[0:8, :], "qd")
                st = dma("q", flat3[:, 1, 0:128],
                         sc_tm[:, :, :].rearrange("p a b -> p (a b)"), "qd")
            em.waits("q", st)
            em.flush(block)
            return nc
        ar.reset(m_phA)

        h_fm = ar.alloc([32, TT2])
        bout_tt = ar.alloc([NCH, TT2])
        w1slot = [ar.alloc([NCH, 128]) for _ in range(2)]
        w1free = [None, None]
        w2slot = [ar.alloc([1024]) for _ in range(2)]
        w2free = [None, None]
        gate_b = ar.alloc([TT2])
        scrtile = ar.alloc([TT2])
        lnsq_ffn = h_fm[:, 0:8, 0:256]

        h_free = None
        bout_free = None
        gateb_free = None

        for tt in range(NTT):
            tc0 = tt * TT2
            bout_first = True
            for ex in range(9):
                e_id = ex - 1
                lds = {}
                def w1_src(fc):
                    if ex == 0:
                        return Ws1[:, bass.ts(fc, 128)].rearrange(
                            "(c p) f -> p c f", p=128)
                    return W1[e_id, :, bass.ts(fc, 128)].rearrange(
                        "(c p) f -> p c f", p=128)
                sl0 = tt * 0
                w1i = 0
                def next_w1():
                    nonlocal w1i
                    s = w1i % 2
                    w1i += 1
                    return s
                s0 = next_w1()
                em.waits("q", w1free[s0])
                lds[0] = (s0, dma("q", w1slot[s0][:, :, :], w1_src(0), "qd"))
                for fc in range(32):
                    if fc + 1 < 32:
                        s1_ = next_w1()
                        em.waits("q", w1free[s1_])
                        lds[fc + 1] = (s1_, dma("q", w1slot[s1_][:, :, :],
                                                w1_src(fc + 1), "qd"))
                    sl, ltk = lds.pop(fc)
                    bk = fc % 2
                    em.waits("t", ltk, ps_busy[bk])
                    mtok = None
                    for c in range(NCH):
                        mtok = mmo(psum[bk][:, :], w1slot[sl][:, c, :],
                                   xres[:, c, tc0:tc0 + TT2], c == 0, c == NCH - 1,
                                   inc="t" if c == NCH - 1 else None)
                    w1free[sl] = mtok
                    ps_ready[bk] = mtok
                    em.wait("s", mtok)
                    bcol = bs1_sb[:, fc:fc + 1] if ex == 0 else b1_sb[:, e_id, fc:fc + 1]
                    etok = em.op("s", lambda e, fc=fc, bk=bk, bc=bcol: e.activation(
                        h_fm[:, fc, :], psum[bk][:, :], ACT.Gelu_apprx_tanh,
                        bias=bc), inc="s")
                    ps_busy[bk] = etok
                h_tok = em.token("s")
                h_free = None

                if ex >= 1:
                    em.waits("t", gateT_tok, ps_busy[2], gateb_free)
                    gbt = mmo(psum[2][:, :], sel8[0:8, bass.ts(e_id, 128)],
                              gateT[0:8, tc0:tc0 + TT2], True, True,
                              inc="t")
                    ps_ready[2] = gbt
                    em.wait("v", gbt)
                    gbv = em.op("v", lambda e: e.tensor_copy(
                        gate_b[:, :], psum[2][:, :]), inc="v")
                    ps_busy[2] = gbv

                def w2_src(fc):
                    if ex == 0:
                        return Ws2[bass.ts(fc, 128), :]
                    return W2[e_id, bass.ts(fc, 128), :]
                w2i = 0
                def next_w2():
                    nonlocal w2i
                    s = w2i % 2
                    w2i += 1
                    return s
                lds2 = {}
                s0 = next_w2()
                em.waits("q", w2free[s0])
                lds2[0] = (s0, dma("q", w2slot[s0][:, :], w2_src(0), "qd"))
                em.waits("t", h_tok)
                for dc in range(NCH):
                    em.waits("t", ps_busy[dc])
                wtoks = [None] * NCH
                for fc in range(32):
                    if fc + 1 < 32:
                        s1_ = next_w2()
                        em.waits("q", w2free[s1_])
                        lds2[fc + 1] = (s1_, dma("q", w2slot[s1_][:, :],
                                                 w2_src(fc + 1), "qd"))
                    sl, ltk = lds2.pop(fc)
                    em.wait("t", ltk)
                    for dc in range(NCH):
                        wtoks[dc] = mmo(psum[dc][:, :], w2slot[sl][:, bass.ts(dc, 128)],
                                        h_fm[:, fc, :], fc == 0, fc == 31,
                                        inc="t" if dc == NCH - 1 else None)
                    w2free[sl] = wtoks[NCH - 1]
                w2_done = wtoks[NCH - 1]
                h_free = w2_done
                for dc in range(NCH):
                    ps_ready[dc] = w2_done
                if ex == 0:
                    em.waits("s", w2_done, bout_free)
                    for dc in range(NCH):
                        etok = em.op("s", lambda e, dc=dc: e.activation(
                            bout_tt[:, dc, :], psum[dc][:, :], ACT.Identity,
                            bias=bs2_sb[:, dc:dc + 1]), inc="s")
                        ps_busy[dc] = etok
                    bout_first = False
                    bout_tok = em.token("s")
                    em.waits("v", bout_tok)
                else:
                    em.waits("v", w2_done)
                    for dc in range(NCH):
                        em.op("v", lambda e, dc=dc: e.scalar_tensor_tensor(
                            scrtile[:, :], psum[dc][:, :],
                            b2_sb[:, e_id, dc:dc + 1], gate_b[:, :],
                            OP.add, OP.mult))
                        etok = em.op("v", lambda e, dc=dc: e.tensor_tensor(
                            bout_tt[:, dc, :], bout_tt[:, dc, :], scrtile[:, :],
                            OP.add), inc="v")
                        ps_busy[dc] = etok
                    gateb_free = etok
            btok = em.token("v")
            em.waits("v", btok)
            rtok = None
            for c in range(NCH):
                rtok = em.op("v", lambda e, c=c, tc0=tc0: e.tensor_tensor(
                    xres[:, c, tc0:tc0 + TT2], xres[:, c, tc0:tc0 + TT2],
                    bout_tt[:, c, :], OP.add), inc="v" if c == NCH - 1 else None)
            bout_free = rtok
            ln_inplace(tc0, 256, rtok, lnsq=lnsq_ffn)
            ln_inplace(tc0 + 256, 256, rtok, lnsq=lnsq_ffn)

        flush_barrier()
        if isinstance(DEBUG_STAGE, str) and DEBUG_STAGE.startswith("dump3:"):
            name = DEBUG_STAGE.split(":")[1]
            flat3 = outT.rearrange("(c p) t -> p c t", p=128)
            if name == "bout":
                bfl = bout_tt[:, :, :].rearrange("p a b -> p (a b)")
                st = dma("q", flat3[:, 0, :], bfl[:, 0:2048], "qd")
                st = dma("q", flat3[:, 1, :], bfl[:, 2048:4096], "qd")
            elif name == "h":
                for i in range(8):
                    st = dma("q", flat3[:, i, :],
                             h_fm[:, :, :].rearrange("p a b -> p (a b)")[
                                 :, i * 2048:(i + 1) * 2048], "qd")
            em.waits("q", st)
            em.flush(block)
            return nc
        ar.reset(m_ffn)

        if DEBUG_STAGE == "s2":
            st = dma("q", outT.rearrange("(c p) t -> p c t", p=128), xres[:, :, :],
                     "qd")
            em.waits("q", st)
            em.flush(block)
            return nc

        QT = 256
        m_att = ar.mark()
        Zfull = ar.alloc([NCH, T])
        K_h = ar.alloc_rows(64, T)
        V_h = ar.alloc_rows(64, T)
        Q_t = ar.alloc_rows(64, QT)
        vht = ar.alloc([16, 65])
        PT = ar.alloc([16, QT])
        kvslot = [ar.alloc([NCH, 64]) for _ in range(2)]
        kvfree = [None, None]
        rb_sb = ar.alloc_rows(64, QT)

        vones_tok = em.op("v", lambda e: e.memset(vht[:, :, 64:65], 1.0), inc="v")
        kh_free = vh_free = qt_free = None
        vht_free = pt_free = None
        z_tok = None

        for hd in range(16):
            hcols = slice(64 * hd, 64 * (hd + 1))
            for pi, (dst, dfree) in enumerate(((K_h, kh_free), (V_h, vh_free))):
                sl = hd * 0 + pi % 2
                em.waits("q", kvfree[sl])
                ltk = dma("q", kvslot[sl][:, :, :],
                          kqv[pi + 1, :, hcols].rearrange("(c p) e -> p c e", p=128),
                          "qd")
                em.waits("t", ltk, dfree)
                for qt4 in range(4):
                    bk = qt4 % 2
                    em.waits("t", ps_busy[bk])
                    mtok = None
                    for c in range(NCH):
                        mtok = mmo(psum[bk][0:64, :], kvslot[sl][:, c, :],
                                   xres[:, c, bass.ts(qt4, 512)], c == 0,
                                   c == NCH - 1, inc="t" if c == NCH - 1 else None)
                    ps_ready[bk] = mtok
                    em.wait("v", mtok)
                    em.op("v", lambda e, d=dst, q4=qt4, bk=bk: e.tensor_copy(
                        d[0:64, bass.ts(q4, 512)], psum[bk][0:64, :]))
                    vtok = vspacer(inc="v")
                    ps_busy[bk] = vtok
                kvfree[sl] = mtok
            kproj_tok = em.token("v")

            em.waits("t", kproj_tok, vht_free)
            for kc in range(16):
                bk = 2 + kc % 2
                em.waits("t", ps_busy[bk])
                ttok = em.op("t", lambda e, kc=kc, bk=bk: e.transpose(
                    psum[bk][0:128, 0:64], V_h[0:64, bass.ts(kc, 128)],
                    ident[0:64, 0:64]), inc="t")
                ps_ready[bk] = ttok
                em.waits("v", ttok, vones_tok)
                em.op("v", lambda e, kc=kc, bk=bk: e.tensor_copy(
                    vht[:, kc, 0:64], psum[bk][0:128, 0:64]))
                vtok = vspacer(inc="v")
                ps_busy[bk] = vtok
            vht_tok = em.token("v")
            vh_free = ps_ready[3]

            slq = 0
            em.waits("q", kvfree[slq])
            qltk = dma("q", kvslot[slq][:, :, :],
                       kqv[0, :, hcols].rearrange("(c p) e -> p c e", p=128), "qd")

            for qt in range(T // QT):
                qcols = slice(QT * qt, QT * (qt + 1))
                em.waits("t", qltk, qt_free, ps_busy[4])
                mtok = None
                for c in range(NCH):
                    mtok = mmo(psum[4][0:64, 0:QT], kvslot[slq][:, c, :],
                               xres[:, c, qcols], c == 0, c == NCH - 1,
                               inc="t" if c == NCH - 1 else None)
                ps_ready[4] = mtok
                em.wait("v", mtok)
                em.op("v", lambda e: e.tensor_scalar(
                    Q_t[0:64, :], psum[4][0:64, 0:QT], 0.125, None, OP.mult))
                qtok = vspacer(inc="v")
                ps_busy[4] = qtok
                em.waits("t", qtok, pt_free)
                for kc in range(16):
                    bk = kc % 2
                    em.waits("t", ps_busy[bk])
                    stok = mmo(psum[bk][0:128, 0:QT], K_h[0:64, bass.ts(kc, 128)],
                               Q_t[0:64, :], True, True, inc="t")
                    ps_ready[bk] = stok
                    em.wait("s", stok)
                    etok = em.op("s", lambda e, kc=kc, bk=bk: e.activation(
                        PT[:, kc, :], psum[bk][0:128, 0:QT], ACT.Exp), inc="s")
                    ps_busy[bk] = etok
                pt_tok = sspacer(inc="s")
                em.waits("t", pt_tok, ps_busy[5])
                ztok = None
                for kc in range(16):
                    ztok = mmo(psum[5][0:65, 0:QT], vht[:, kc, 0:65], PT[:, kc, :],
                               kc == 0, kc == 15, inc="t" if kc == 15 else None)
                ps_ready[5] = ztok
                pt_free = ztok
                em.wait("v", ztok)
                em.op("v", lambda e: e.reciprocal(
                    row_a[0:1, 0:QT], psum[5][64:65, 0:QT]))
                rtok = vspacer(inc="v")
                em.waits("t", rtok, ps_busy[6])
                btok = mmo(psum[6][0:64, 0:QT], ones_row[:, 0:64], row_a[0:1, 0:QT],
                           True, True, inc="t")
                ps_ready[6] = btok
                em.wait("v", btok)
                rbt = em.op("v", lambda e: e.tensor_copy(
                    rb_sb[:, :], psum[6][0:64, 0:QT]), inc="v")
                ps_busy[6] = rbt
                vspacer()
                ztile = Zfull[64 * (hd % 2):64 * (hd % 2) + 64, hd // 2, qcols]
                vtok = em.op("v", lambda e, zt=ztile: e.tensor_tensor(
                    zt, psum[5][0:64, 0:QT], rb_sb[:, :], OP.mult),
                    inc="v")
                ps_busy[5] = vtok
                qt_free = ps_ready[1]
            kvfree[slq] = em.token("t")
            kh_free = em.token("t")
        z_tok = em.token("v")

        em.waits("v", z_tok)
        rtok = None
        for c in range(NCH):
            rtok = em.op("v", lambda e, c=c: e.tensor_tensor(
                xres[:, c, :], xres[:, c, :], Zfull[:, c, :], OP.add),
                inc="v" if c == NCH - 1 else None)
        lnsq_att = Zfull[:, 0:8, 0:256]
        for blk in range(8):
            ln_inplace(blk * 256, 256, rtok, lnsq=lnsq_att)

        flush_barrier()
        ar.reset(m_att)

        if DEBUG_STAGE == "s3":
            st = dma("q", outT.rearrange("(c p) t -> p c t", p=128), xres[:, :, :],
                     "qd")
            em.waits("q", st)
            em.flush(block)
            return nc

        m_mo = ar.mark()
        combden = ar.alloc([16])
        tmpacc = ar.alloc([1])
        rcd = ar.alloc([16])
        phisl = ar.alloc([NCH, 512])
        expsb = [ar.alloc([512]) for _ in range(2)]
        exp_free = [None, None]

        cd0 = em.op("v", lambda e: e.memset(combden[:, :], 0.0), inc="v")

        phi_free = None
        acc_free = cd0
        for npt in range(4):
            em.waits("q", phi_free)
            phitok = dma("q", phisl[:, :, :],
                         phi[:, bass.ts(npt, 512)].rearrange(
                             "(c p) e -> p c e", p=128), "qd")
            for tc in range(16):
                bk = tc % 2
                em.waits("t", phitok, ps_busy[bk])
                mtok = None
                for c in range(NCH):
                    mtok = mmo(psum[bk][:, :], xres[:, c, bass.ts(tc, 128)],
                               phisl[:, c, :], c == 0, c == NCH - 1,
                               inc="t" if c == NCH - 1 else None)
                ps_ready[bk] = mtok
                sl = tc % 2
                em.waits("s", mtok, exp_free[sl], acc_free)
                em.op("s", lambda e, bk=bk, sl=sl: e.activation(
                    expsb[sl][:, :], psum[bk][:, :], ACT.Exp,
                    accum_out=tmpacc[:, :]))
                etok = sspacer(inc="s")
                ps_busy[bk] = etok
                em.waits("v", etok)
                atok = em.op("v", lambda e, tc=tc: e.tensor_tensor(
                    combden[:, tc:tc + 1], combden[:, tc:tc + 1], tmpacc[:, :],
                    OP.add), inc="v")
                acc_free = atok
                em.waits("q", etok)
                dtok = dma("q", expl_hbm[bass.ts(tc, 128), bass.ts(npt, 512)],
                           expsb[sl][:, :], "qd")
                exp_free[sl] = dtok
            phi_free = mtok
        em.waits("v", acc_free)
        vspacer()
        em.op("v", lambda e: e.reciprocal(rcd[:, :], combden[:, :]))
        rcdt = vspacer(inc="v")

        exp_all_tok = em.token("qd")
        em.waits("g", exp_all_tok)
        cc2 = em.op("g", lambda e: e.collective_compute(
            "AllReduce", OP.add, replica_groups=[list(range(n_cores))],
            ins=[expl_hbm.opt()], outs=[den_hbm.opt()]), inc="cc")

        xttile = ar.alloc([128])
        for tc in range(16):
            for dcb in range(NCH):
                bk = 4 + (tc * NCH + dcb) % 2
                em.waits("t", ps_busy[bk])
                ttok = em.op("t", lambda e, tc=tc, dcb=dcb, bk=bk: e.transpose(
                    psum[bk][:, 0:128], xres[:, dcb, bass.ts(tc, 128)],
                    ident[:, :]), inc="t")
                ps_ready[bk] = ttok
                em.wait("v", ttok)
                vtok = em.op("v", lambda e, bk=bk: e.tensor_copy(
                    xttile[:, :], psum[bk][:, 0:128]), inc="v")
                ps_busy[bk] = vtok
                em.waits("q", vtok)
                xtok = dma("q", xT_hbm[bass.ts(tc, 128), bass.ts(dcb, 128)],
                           xttile[:, :], "qd")
                em.waits("v", xtok)
        em.waits("g", cc2)
        if isinstance(DEBUG_STAGE, str) and DEBUG_STAGE.startswith("dump4:"):
            flat3 = outT.rearrange("(c p) t -> p c t", p=128)
            st = None
            for tcq in range(16):
                st = dma("q", flat3[:, tcq % 8, (tcq // 8) * 1024:(tcq // 8) * 1024 + 1024],
                         den_hbm[bass.ts(tcq, 128), 0:1024], "qd")
            em.waits("q", st)
            em.flush(block)
            return nc
        xtw_tok = em.token("qd")
        em.waits("q", xtw_tok)

        disp_sb = ar.alloc([16, 128])
        slots_c = ar.alloc([NCH, 128])
        slotsT = ar.alloc([NCH, 128])
        y_np = [ar.alloc([NCH, 128]) for _ in range(4)]
        combT = [ar.alloc([T]) for _ in range(4)]
        wsl = [ar.alloc([NCH, 128]) for _ in range(2)]
        wfree = [None, None]
        xtsl = [ar.alloc([512]) for _ in range(2)]
        xtfree = [None, None]
        etile = ar.alloc([128])
        dtile = ar.alloc([128])
        ytmp = ar.alloc([32])
        y_free = [None] * 4
        combT_free = [None] * 4
        ed_free_v = None
        ed_free_t = None

        for g4 in range(4):
            for ncl in range(4):
                npc = g4 * 4 + ncl
                npcols = slice(128 * npc, 128 * (npc + 1))
                em.waits("v", combT_free[ncl])
                disp_tok = None
                for tc in range(16):
                    em.waits("g", ed_free_v, ed_free_t)
                    etok_d = dma("g", etile[:, :],
                                 expl_hbm[bass.ts(tc, 128), npcols], "gd")
                    dtok_d = dma("g", dtile[:, :],
                                 den_hbm[bass.ts(tc, 128), npcols], "gd")
                    em.waits("v", etok_d, dtok_d)
                    em.op("v", lambda e: e.reciprocal(dtile[:, :], dtile[:, :]))
                    vspacer()
                    em.op("v", lambda e, tc=tc: e.tensor_tensor(
                        disp_sb[:, tc, :], etile[:, :], dtile[:, :], OP.mult))
                    em.op("v", lambda e, tc=tc: e.tensor_scalar(
                        etile[:, :], etile[:, :], rcd[:, tc:tc + 1], None,
                        OP.mult))
                    ctok = vspacer(inc="v")
                    ed_free_v = ctok
                    bk = 6 + tc % 2
                    em.waits("t", ctok, ps_busy[bk])
                    ttok = em.op("t", lambda e, bk=bk: e.transpose(
                        psum[bk][:, 0:128], etile[:, :], ident[:, :]), inc="t")
                    ps_ready[bk] = ttok
                    ed_free_t = ttok
                    em.wait("v", ttok)
                    vtok = em.op("v", lambda e, tc=tc, bk=bk, ncl=ncl: e.tensor_copy(
                        combT[ncl][:, bass.ts(tc, 128)], psum[bk][:, 0:128]),
                        inc="v")
                    ps_busy[bk] = vtok
                    disp_tok = vtok
                em.waits("t", disp_tok)
                for dh in range(2):
                    bk = dh
                    em.waits("t", ps_busy[bk])
                    stok = None
                    for tc in range(16):
                        sl = (dh * 16 + tc) % 2
                        em.waits("q", xtfree[sl])
                        xltk = dma("q", xtsl[sl][:, :],
                                   xT_hbm[bass.ts(tc, 128), bass.ts(dh, 512)],
                                   "qd")
                        em.waits("t", xltk)
                        stok = mmo(psum[bk][:, :], disp_sb[:, tc, :],
                                   xtsl[sl][:, :], tc == 0, tc == 15, inc="t")
                        xtfree[sl] = stok
                    ps_ready[bk] = stok
                    em.wait("v", stok)
                    em.op("v", lambda e, dh=dh, bk=bk: e.tensor_copy(
                        slots_c[:, :, :].rearrange("p c e -> p (c e)")[
                            :, bass.ts(dh, 512)], psum[bk][:, :]))
                    vtok = vspacer(inc="v")
                    ps_busy[bk] = vtok
                slots_tok = vtok
                em.waits("t", slots_tok)
                stT_tok = None
                for dcb in range(NCH):
                    bk = 2 + dcb % 2
                    em.waits("t", ps_busy[bk])
                    ttok = em.op("t", lambda e, dcb=dcb, bk=bk: e.transpose(
                        psum[bk][:, 0:128], slots_c[:, dcb, :], ident[:, :]),
                        inc="t")
                    ps_ready[bk] = ttok
                    em.wait("v", ttok)
                    stT_tok = em.op("v", lambda e, dcb=dcb, bk=bk: e.tensor_copy(
                        slotsT[:, dcb, :], psum[bk][:, 0:128]), inc="v")
                    ps_busy[bk] = stT_tok
                em.waits("t", stT_tok)
                em.waits("v", y_free[ncl])
                wli = 0
                for nl in range(4):
                    n_id = npc * 4 + nl
                    for ec in range(NCH):
                        sl = wli % 2
                        wli += 1
                        em.waits("q", wfree[sl])
                        wltk = dma("q", wsl[sl][:, :, :],
                                   Wmo[n_id, :, bass.ts(ec, 128)].rearrange(
                                       "(c p) e -> p c e", p=128), "qd")
                        bk = 4 + (nl * NCH + ec) % 2
                        em.waits("t", wltk, ps_busy[bk])
                        ytok = None
                        for dc in range(NCH):
                            ytok = mmo(psum[bk][:, 0:32], wsl[sl][:, dc, :],
                                       slotsT[:, dc, 32 * nl:32 * (nl + 1)],
                                       dc == 0, dc == NCH - 1,
                                       inc="t" if dc == NCH - 1 else None)
                        wfree[sl] = ytok
                        ps_ready[bk] = ytok
                        em.wait("v", ytok)
                        em.op("v", lambda e, n_id=n_id, ec=ec, bk=bk:
                              e.tensor_scalar(
                                  ytmp[:, :], psum[bk][:, 0:32],
                                  bmo_sb[:, n_id, ec:ec + 1], None,
                                  OP.add))
                        vtok = vspacer(inc="v")
                        ps_busy[bk] = vtok
                        bk2 = 6 + (nl * NCH + ec) % 2
                        em.waits("t", vtok, ps_busy[bk2])
                        ttok = em.op("t", lambda e, bk2=bk2: e.transpose(
                            psum[bk2][0:32, 0:128], ytmp[:, :],
                            ident[:, :]), inc="t")
                        ps_ready[bk2] = ttok
                        em.wait("v", ttok)
                        yvt = em.op("v", lambda e, nl=nl, ec=ec, bk2=bk2, ncl=ncl:
                                    e.tensor_copy(
                                        y_np[ncl][32 * nl:32 * (nl + 1), ec, :],
                                        psum[bk2][0:32, 0:128]), inc="v")
                        ps_busy[bk2] = yvt
            ygrp_tok = vspacer(inc="v")
            em.waits("t", ygrp_tok)
            out_last = None
            for tt in range(4):
                for dcb in range(NCH):
                    bk = (tt * NCH + dcb) % 2
                    em.waits("t", ps_busy[bk])
                    otok = None
                    for ncl in range(4):
                        otok = mmo(psum[bk][:, :], y_np[ncl][:, dcb, :],
                                   combT[ncl][:, bass.ts(tt, 512)],
                                   ncl == 0, ncl == 3,
                                   inc="t" if ncl == 3 else None)
                    ps_ready[bk] = otok
                    em.wait("v", otok)
                    vtok = em.op("v", lambda e, tt=tt, dcb=dcb, bk=bk:
                                 e.tensor_tensor(
                                     xres[:, dcb, bass.ts(tt, 512)],
                                     xres[:, dcb, bass.ts(tt, 512)],
                                     psum[bk][:, :], OP.add), inc="v")
                    ps_busy[bk] = vtok
                    out_last = vtok
            grp_end = em.token("t")
            for ncl in range(4):
                y_free[ncl] = grp_end
                combT_free[ncl] = grp_end

        if isinstance(DEBUG_STAGE, str) and DEBUG_STAGE.startswith("dump5:"):
            vspacer()
            vspacer()
            dtk = em.op("v", lambda e: e.tensor_copy(vspc[:, :], vspc[:, :]), inc="v")
            em.waits("q", dtk)
            flat3 = outT.rearrange("(c p) t -> p c t", p=128)
            st = dma("q", flat3[:, 0, 0:1024],
                     slots_c[:, :, :].rearrange("p a b -> p (a b)"), "qd")
            st = dma("q", flat3[:, 1, 0:1024],
                     y_np[3][:, :, :].rearrange("p a b -> p (a b)"), "qd")
            st = dma("q", flat3[:, 2, :], combT[3][:, :], "qd")
            st = dma("q", flat3[:, 3, 0:1024],
                     slotsT[:, :, :].rearrange("p a b -> p (a b)"), "qd")
            em.waits("q", st)
            em.flush(block)
            return nc

        if isinstance(DEBUG_STAGE, str) and DEBUG_STAGE.startswith("dump6:"):
            vspacer()
            vspacer()
            dtk = em.op("v", lambda e: e.tensor_copy(vspc[:, :], vspc[:, :]), inc="v")
            em.waits("q", dtk)
            st = dma("q", outT.rearrange("(c p) t -> p c t", p=128), xres[:, :, :],
                     "qd")
            em.waits("q", st)
            em.flush(block)
            return nc

        fin_tok = em.token("v")
        lnsq_mo = ar.alloc([NCH, 256])
        for blk in range(8):
            ln_inplace(blk * 256, 256, fin_tok, lnsq=lnsq_mo)

        vspacer()
        vspacer()
        fin = em.op("v", lambda e: e.tensor_copy(vspc[:, :], vspc[:, :]), inc="v")
        em.waits("q", fin)
        st = dma("q", outT.rearrange("(c p) t -> p c t", p=128), xres[:, :, :],
                 "qd")
        em.waits("q", st)
        em.flush(block)
        return nc

    return nc


def kernel(**inputs):
    I = {k: np.asarray(v, dtype=np.float32) for k, v in inputs.items()}
    n = 8
    x = I["L0_attn_x"]
    perm = np.concatenate([np.arange(0, T, 2), np.arange(1, T, 2)])
    xT = np.ascontiguousarray(np.transpose(x[:, perm, :], (0, 2, 1)))
    base = {
        "w_inT": np.ascontiguousarray(I["L0_attn_w_in"].T),
        "b_in": I["L0_attn_b_in"],
        "A": I["L0_attn_A"], "Bm": I["L0_attn_Bm"], "Cm": I["L0_attn_Cm"],
        "Dv": I["L0_attn_Dv"],
        "w_outT": np.ascontiguousarray(I["L0_attn_w_out"].T),
        "b_out": I["L0_attn_b_out"], "g_norm": I["L0_attn_g_norm"],
        "gwT": np.ascontiguousarray(I["L0_ffn_gate_weight"].T),
        "W1": I["L0_ffn_W1"], "b1": I["L0_ffn_b1"],
        "W2": I["L0_ffn_W2"], "b2": I["L0_ffn_b2"],
        "Ws1": I["L0_ffn_Ws1"], "bs1": I["L0_ffn_bs1"],
        "Ws2": I["L0_ffn_Ws2"], "bs2": I["L0_ffn_bs2"],
        "kqv": I["L1_attn_kernel"],
        "phi": np.ascontiguousarray(I["L1_ffn_phi"].reshape(1024, 2048)),
        "Wmo": I["L1_ffn_W"], "bmo": I["L1_ffn_b"],
        "ident_in": np.eye(128, dtype=np.float32),
        "sel8_in": np.repeat(np.eye(8, dtype=np.float32), 128, axis=1),
    }
    nc = build_nc(n)
    in_maps = [dict(base, xT=np.ascontiguousarray(xT[c])) for c in range(n)]
    res = run_bass_kernel_spmd(nc, in_maps, core_ids=list(range(n)))
    out = np.stack([np.asarray(res.results[c]["outT"]).T for c in range(n)])
    if not (isinstance(DEBUG_STAGE, str) and DEBUG_STAGE.startswith("dump")):
        inv = np.argsort(perm)
        out = out[:, inv, :]
    return np.ascontiguousarray(out).astype(np.float32)
